# version 03 (final)
# baseline (speedup 1.0000x reference)
import numpy as np
import ml_dtypes
import concourse.bass as bass
import concourse.mybir as mybir
from concourse.bass_utils import run_bass_kernel_spmd
from contextlib import ExitStack

F32 = mybir.dt.float32
I32 = mybir.dt.int32
BF16 = mybir.dt.bfloat16
AF = mybir.ActivationFunctionType
ALU = mybir.AluOpType
BF = ml_dtypes.bfloat16

NC = 8
B, T, C = 4, 2048, 2048
TOK = 1024
HALO = 256
THL = TOK + HALO
NT = TOK // 128
NTH = THL // 128
CT = C // 128
FF = 8192
FT = FF // 128
MAGIC = 0x5F3759DF
MAGF = float(np.float32(1.5 * 2 ** 23))
EPS = 1e-5
RMS_EPS = float(np.finfo(np.float32).eps)
INV_SQRT_HD = float(1.0 / np.sqrt(128.0))

DEBUG_STOP = None


class Ctx:
    def __init__(self, nc):
        self.nc = nc
        self.ops = {e: [] for e in ("sync", "gpsimd", "vector", "scalar", "tensor")}
        self.sems = {}
        self.cnt = {}

    def sem(self, es, name):
        h = es.enter_context(self.nc.semaphore(name))
        self.sems[name] = h
        self.cnt[name] = 0

    def on(self, eng, fn):
        self.ops[eng].append(fn)

    def inc(self, sem, n):
        self.cnt[sem] += n
        return (sem, self.cnt[sem])

    def tok(self, sem):
        return (sem, self.cnt[sem])

    def wait(self, eng, token):
        if token is None:
            return
        sem, val = token
        h = self.sems[sem]
        self.on(eng, lambda e, h=h, v=val: e.wait_ge(h, v))


def build_program():
    nc = bass.Bass("TRN2", target_bir_lowering=False, debug=False)
    es = ExitStack()
    ctx = Ctx(nc)
    T_ = lambda nm, shp, dt=F32: es.enter_context(nc.sbuf_tensor(nm, shp, dt))

    def din(name, shape, dt=F32):
        return nc.dram_tensor(name, list(shape), dt, kind="ExternalInput").ap()

    xh = din("xh", [THL, C])
    hmask_d = din("hmask", [128, THL])
    pmbB_d = din("pmbB", [128, 1])
    wsA_d = din("wsA", [128, 1])
    wsB_d = din("wsB", [128, 1])
    v16_d = {nm: din(nm, [128, CT]) for nm in ("tmk", "tmv", "tmr", "tmg", "td", "tf")}
    rows_d = {nm: din(nm, [1, C]) for nm in ("lng_row", "lnb_row", "gq_row", "gk_row")}
    wT_d = {nm: din(nm + "T", [C, C])
            for nm in ("wr", "wk0", "wv0", "wg", "wo0", "wq", "wk1", "wv1", "wo1")}
    fc_sh = din("fc_sh", [1024, C])
    pj_sh = din("pj_sh", [256, FF])
    w1_sh = din("w1T_sh", [256, FF])
    w2_sh = din("w2T_sh", [256, FF])
    w3_sh = din("w3T_sh", [1024, C])
    out_d = nc.dram_tensor("out", [TOK, C], F32, kind="ExternalOutput").ap()

    xres = nc.dram_tensor("xres", [TOK, C], F32).ap()
    zsp = nc.dram_tensor("zsp", [TOK, C], F32).ap()
    xnT_sp = nc.dram_tensor("xnT_sp", [C, THL], BF16).ap()
    wkv_sp = nc.dram_tensor("wkv_sp", [C, TOK], BF16).ap()
    ag_fc_in = nc.dram_tensor("ag_fc_in", [C, 1024], BF16)
    ag_fc_out = nc.dram_tensor("ag_fc_out", [NC * C, 1024], BF16)
    ag_pj_in = nc.dram_tensor("ag_pj_in", [FF, 256], BF16)
    ag_pj_out = nc.dram_tensor("ag_pj_out", [NC * FF, 256], BF16)
    ag_w1_in = nc.dram_tensor("ag_w1_in", [256, FF], BF16)
    ag_w1_out = nc.dram_tensor("ag_w1_out", [NC * 256, FF], BF16)
    ag_w2_in = nc.dram_tensor("ag_w2_in", [256, FF], BF16)
    ag_w2_out = nc.dram_tensor("ag_w2_out", [NC * 256, FF], BF16)
    ag_w3_in = nc.dram_tensor("ag_w3_in", [1024, C], BF16)
    ag_w3_out = nc.dram_tensor("ag_w3_out", [NC * 1024, C], BF16)
    ar_in = nc.dram_tensor("ar_in", [1, 4], F32)
    ar_out = nc.dram_tensor("ar_out", [1, 4], F32)
    kv_in = nc.dram_tensor("kv_in", [4096, 1024], BF16)
    kv_out = nc.dram_tensor("kv_out", [2 * 4096, 1024], BF16)

    ident = T_("ident", [128, 128], BF16)
    wedgeA = T_("wedgeA", [128, 128], BF16)
    wedgeB = T_("wedgeB", [128, 128], BF16)
    wedgeAb = T_("wedgeAb", [128, 128], BF16)
    wedgeBb = T_("wedgeBb", [128, 128], BF16)
    fmAb = T_("fmAb", [128, 128], BF16)
    fmBb = T_("fmBb", [128, 128], BF16)
    ones_row_f = T_("ones_row_f", [1, 128], F32)
    ones128_f = T_("ones128_f", [128, 1], F32)
    ones_col_b = T_("ones_col_b", [128, 1], BF16)
    hmask = T_("hmask_t", [128, THL])
    pmbB = T_("pmbB_t", [128, 1])
    bias20 = T_("bias20", [128, 1])
    wsA = T_("wsA_t", [128, 1])
    wsB = T_("wsB_t", [128, 1])
    v16 = {nm: T_(nm + "_t", [128, CT]) for nm in ("tmk", "tmv", "tmr", "tmg", "td", "tf")}
    wdec = T_("wdec", [128, CT])
    eu = T_("eu", [128, CT])
    rowA = T_("rowA", [1, C])
    rowB = T_("rowB", [1, C])
    sscale = T_("sscale", [128, 4])
    st_s1 = T_("st_s1", [128, 16])
    st_s2 = T_("st_s2", [128, 16])
    st_mu = T_("st_mu", [128, 16])
    st_var = T_("st_var", [128, 16])
    st_rs = T_("st_rs", [128, 16])
    st_mu2 = T_("st_mu2", [128, 16])
    st_rs2 = T_("st_rs2", [128, 16])
    st_i1 = es.enter_context(nc.sbuf_tensor("st_i1", [128, 16], I32))
    st_i2 = es.enter_context(nc.sbuf_tensor("st_i2", [128, 16], I32))
    st_t1 = T_("st_t1", [128, 16])
    st_t2 = T_("st_t2", [128, 16])
    st_t3 = T_("st_t3", [128, 16])
    st_hv = T_("st_hv", [128, 16])
    junk_bf = T_("junk_bf", [128, FF], BF16)
    xtile = T_("xtile", [128, C])
    xnbf = T_("xnbf", [128, C], BF16)
    ztmpA = T_("ztmpA", [128, C])
    ztmpB = T_("ztmpB", [128, C])

    for s in ("ld", "gd", "vs", "ss", "ts", "cc", "st"):
        ctx.sem(es, s)

    def V(fn):
        ctx.on("vector", fn)

    def S(fn):
        ctx.on("scalar", fn)

    def vbar():
        t = ctx.inc("vs", 1)
        ctx.on("vector", lambda e: e.maybe_drain_then_inc((ctx.sems["vs"], 1)))
        ctx.on("vector", lambda e, v=t[1]: e.wait_ge(ctx.sems["vs"], v))
        return t

    def vdone():
        t = ctx.inc("vs", 1)
        ctx.on("vector", lambda e: e.maybe_drain_then_inc((ctx.sems["vs"], 1)))
        return t

    def sdone():
        t = ctx.inc("ss", 1)
        ctx.on("scalar", lambda e: e.maybe_drain_then_inc((ctx.sems["ss"], 1)))
        return t

    def gdone():
        t = ctx.inc("gd", 1)
        ctx.on("gpsimd", lambda e: e.maybe_drain_then_inc((ctx.sems["gd"], 1)))
        return t

    def dma(queue, outap, inap, sem="ld", wait=None):
        if wait is not None:
            ctx.wait(queue, wait)
        t = ctx.inc(sem, 16)
        h = ctx.sems[sem]
        ctx.on(queue, lambda e, o=outap, i=inap, h=h: e.dma_start(out=o, in_=i).then_inc(h, 16))
        return t

    def dma3(queue, sbuf_ap, dram_ap, cw, sem="ld", wait=None, to_dram=False):
        d3 = dram_ap.rearrange("(j p) c -> p j c", p=128)
        s3 = sbuf_ap.rearrange("p (j c) -> p j c", c=cw)
        if to_dram:
            return dma(queue, d3, s3, sem=sem, wait=wait)
        return dma(queue, s3, d3, sem=sem, wait=wait)

    def mm(outap, lhsT, rhs, start, stop):
        ctx.on("tensor", lambda e, o=outap, l=lhsT, r=rhs, a=start, b=stop:
               e.matmul(o, l, r, start=a, stop=b))

    def tmark():
        t = ctx.inc("ts", 1)
        ctx.on("tensor", lambda e: e.maybe_drain_then_inc((ctx.sems["ts"], 1)))
        return t

    def phase_barrier():
        t_v = vdone()
        t_s = sdone()
        t_t = tmark()
        t_g = gdone()
        t_ld = ctx.tok("ld")
        t_gd = ctx.tok("gd")
        t_st = ctx.tok("st")
        for eng in ("sync", "gpsimd", "vector", "scalar", "tensor"):
            for tk in (t_v, t_s, t_t, t_g, t_ld, t_gd, t_st):
                ctx.wait(eng, tk)

    def quake_rsqrt(n, src_ap, dst_ap):
        sl = (slice(None), slice(0, n))
        V(lambda e: e.tensor_scalar(out=st_i1[sl], in0=src_ap.bitcast(I32),
                                    scalar1=1, scalar2=None, op0=ALU.logical_shift_right))
        vbar()
        V(lambda e: e.tensor_scalar(out=st_i2[sl], in0=st_i1[sl], scalar1=MAGIC,
                                    scalar2=-1.0, op0=ALU.subtract, op1=ALU.mult))
        V(lambda e: e.tensor_scalar(out=st_hv[sl], in0=src_ap, scalar1=0.5,
                                    scalar2=None, op0=ALU.mult))
        vbar()
        V(lambda e: e.tensor_scalar(out=dst_ap, in0=st_i2[sl].bitcast(F32),
                                    scalar1=1.0, scalar2=None, op0=ALU.mult))
        vbar()
        for _ in range(3):
            V(lambda e: e.tensor_tensor(out=st_t1[sl], in0=dst_ap, in1=dst_ap, op=ALU.mult))
            vbar()
            V(lambda e: e.tensor_tensor(out=st_t2[sl], in0=st_t1[sl], in1=st_hv[sl], op=ALU.mult))
            vbar()
            V(lambda e: e.tensor_scalar(out=st_t3[sl], in0=st_t2[sl], scalar1=-1.0,
                                        scalar2=1.5, op0=ALU.mult, op1=ALU.add))
            vbar()
            V(lambda e: e.tensor_tensor(out=st_t1[sl], in0=dst_ap, in1=st_t3[sl], op=ALU.mult))
            vbar()
            V(lambda e: e.tensor_scalar(out=dst_ap, in0=st_t1[sl], scalar1=1.0,
                                        scalar2=None, op0=ALU.mult))
            vbar()

    def stats_accum(src_ap, col, n_free):
        V(lambda e, c=col, n=n_free, s=src_ap: e.tensor_scalar(
            out=junk_bf[:, 0:n], in0=s, scalar1=1.0, scalar2=0.0,
            op0=ALU.mult, op1=ALU.add, accum_out=st_s1[:, c:c + 1]))
        V(lambda e, c=col, n=n_free, s=src_ap: e.scalar_tensor_tensor(
            out=junk_bf[:, 0:n], in0=s, scalar=1.0, in1=s,
            op0=ALU.mult, op1=ALU.mult, accum_out=st_s2[:, c:c + 1]))

    def stats_finish(ntiles, eps, n_free, mu_out, rs_out):
        nsl = (slice(None), slice(0, ntiles))
        vbar()
        V(lambda e: e.tensor_scalar(out=mu_out[nsl], in0=st_s1[nsl],
                                    scalar1=1.0 / n_free, scalar2=None, op0=ALU.mult))
        vbar()
        V(lambda e: e.tensor_tensor(out=st_t1[nsl], in0=mu_out[nsl], in1=mu_out[nsl],
                                    op=ALU.mult))
        V(lambda e: e.tensor_scalar(out=st_var[nsl], in0=st_s2[nsl],
                                    scalar1=1.0 / n_free, scalar2=None, op0=ALU.mult))
        vbar()
        V(lambda e: e.tensor_tensor(out=st_t2[nsl], in0=st_var[nsl], in1=st_t1[nsl],
                                    op=ALU.subtract))
        vbar()
        V(lambda e: e.tensor_scalar(out=st_var[nsl], in0=st_t2[nsl], scalar1=eps,
                                    scalar2=None, op0=ALU.add))
        vbar()
        quake_rsqrt(ntiles, st_var[nsl], rs_out[nsl])

    tp_state = {"pa": None, "pb": None, "last": [None, None], "flip": 0}

    def tp_begin(pa, pb, guard=None):
        tp_state.update(pa=pa, pb=pb, last=[guard, guard], flip=0)

    def tp_chunk(in_ap, out_ap):
        f = tp_state["flip"]
        pt = tp_state["pa"] if f == 0 else tp_state["pb"]
        ctx.wait("tensor", tp_state["last"][f])
        t = ctx.inc("ts", 1)
        ctx.on("tensor", lambda e, i=in_ap, p=pt: e.transpose(
            p[:, 0:128], i, ident[:, :]).then_inc(ctx.sems["ts"], 1))
        ctx.wait("scalar", t)
        S(lambda e, o=out_ap, p=pt: e.copy(out=o, in_=p[:, 0:128]))
        tp_state["last"][f] = sdone()
        tp_state["flip"] = 1 - f

    def tp_flush():
        return tp_state["last"]

    def bcast_rows(pb512, jobs):
        ctx.wait("tensor", v_ones)
        ve_last = None
        for wait_tok, src, dst in jobs:
            ctx.wait("tensor", wait_tok)
            for j in range(C // 512):
                ctx.wait("tensor", ve_last)
                t = ctx.inc("ts", 1)
                ctx.on("tensor", lambda e, s=src, j=j, p=pb512: e.matmul(
                    p[:, :], ones_row_f[0:1, :], s[0:1, 512 * j:512 * (j + 1)],
                    start=True, stop=True).then_inc(ctx.sems["ts"], 1))
                ctx.wait("vector", t)
                V(lambda e, d=dst, j=j, p=pb512: e.tensor_copy(
                    d[:, 512 * j:512 * (j + 1)], p[:, :]))
                ve_last = vdone()
        return ve_last

    ctx.on("gpsimd", lambda e: e.memset(ident[:, :], 0.0))
    ctx.on("gpsimd", lambda e: e.affine_select(
        out=ident[:, :], in_=ident[:, :], compare_op=ALU.not_equal, fill=1.0,
        base=0, pattern=[[-1, 128]], channel_multiplier=1))
    ctx.on("gpsimd", lambda e: e.memset(wedgeA[:, :], 0.0))
    ctx.on("gpsimd", lambda e: e.affine_select(
        out=wedgeA[:, :], in_=wedgeA[:, :], compare_op=ALU.is_ge, fill=-60000.0,
        base=0, pattern=[[1, 128]], channel_multiplier=-1))
    g_const = gdone()

    V(lambda e: e.memset(ones_row_f[:, :], 1.0))
    V(lambda e: e.memset(ones128_f[:, :], 1.0))
    V(lambda e: e.memset(ones_col_b[:, :], 1.0))
    V(lambda e: e.memset(bias20[:, :], -20.0))
    v_ones = vdone()

    t0 = dma("sync", hmask[:, :], hmask_d)
    t0 = dma("sync", pmbB[:, :], pmbB_d)
    t0 = dma("sync", wsA[:, :], wsA_d)
    t0 = dma("sync", wsB[:, :], wsB_d)
    for nm in v16:
        t0 = dma("sync", v16[nm][:, :], v16_d[nm])
    t_small = t0

    ctx.wait("vector", t_small)
    ctx.wait("vector", g_const)
    V(lambda e: e.tensor_scalar(out=wedgeB[:, :], in0=wedgeA[:, :],
                                scalar1=wsB[:, 0:1], scalar2=None, op0=ALU.mult))
    vbar()
    V(lambda e: e.tensor_scalar(out=wedgeA[:, :], in0=wedgeB[:, :].bitcast(F32) if False else st_t3[:, 0:1], scalar1=1.0, scalar2=None, op0=ALU.mult)) if False else None
    V(lambda e: e.tensor_scalar(out=st_t3[:, 0:16], in0=st_t3[:, 0:16], scalar1=1.0, scalar2=None, op0=ALU.mult)) if False else None
    V(lambda e: e.tensor_scalar(out=wedgeBb[:, :], in0=wedgeB[:, :], scalar1=1.0,
                                scalar2=None, op0=ALU.mult))
    vbar()
    V(lambda e: e.tensor_scalar(out=st_hv[:, 0:1], in0=wedgeA[:, 0:1], scalar1=1.0,
                                scalar2=None, op0=ALU.mult)) if False else None
    V(lambda e: e.tensor_scalar(out=wedgeAb[:, :], in0=wedgeA[:, :],
                                scalar1=wsA[:, 0:1], scalar2=None, op0=ALU.mult))
    vbar()
    V(lambda e: e.memset(ztmpA[:, 0:128], -60000.0))
    vbar()
    V(lambda e: e.tensor_scalar(out=fmAb[:, :], in0=ztmpA[:, 0:128],
                                scalar1=wsA[:, 0:1], scalar2=None, op0=ALU.mult))
    V(lambda e: e.tensor_scalar(out=fmBb[:, :], in0=ztmpA[:, 0:128],
                                scalar1=wsB[:, 0:1], scalar2=None, op0=ALU.mult))
    v_wedge = vbar()

    S(lambda e: e.preload_act_table(AF.Exp))
    ctx.wait("scalar", t_small)
    S(lambda e: e.activation(out=st_hv[:, 0:CT], in_=v16["td"][:, :], func=AF.Exp))
    s_e1 = sdone()
    ctx.wait("scalar", s_e1)
    S(lambda e: e.activation(out=wdec[:, :], in_=st_hv[:, 0:CT], func=AF.Exp, scale=-1.0))
    S(lambda e: e.activation(out=eu[:, :], in_=v16["tf"][:, :], func=AF.Exp))
    s_wdec = sdone()
    ctx.wait("vector", s_wdec)

    es2 = ExitStack()
    pp0 = es2.enter_context(nc.psum_tensor("pp0", [128, 512], F32))
    ptp_a = es2.enter_context(nc.psum_tensor("ptp_a", [128, 1024], BF16))
    ptp_b = es2.enter_context(nc.psum_tensor("ptp_b", [128, 1024], BF16))
    prep_in = es2.enter_context(nc.sbuf_tensor("prep_in", [128, FF], F32))
    prep_bf = es2.enter_context(nc.sbuf_tensor("prep_bf", [128, FF], BF16))
    prep_q = es2.enter_context(nc.sbuf_tensor("prep_q", [128, FF], F32))
    prep_sc = es2.enter_context(nc.sbuf_tensor("prep_sc", [128, 128], F32))
    prep_scb = es2.enter_context(nc.sbuf_tensor("prep_scb", [128, 128], BF16))
    prep_rc = es2.enter_context(nc.sbuf_tensor("prep_rc", [128, 128], F32))
    acc_abs = es2.enter_context(nc.sbuf_tensor("acc_abs", [128, 4], F32))
    acc_row = es2.enter_context(nc.sbuf_tensor("acc_row", [1, 4], F32))
    tqbuf = es2.enter_context(nc.sbuf_tensor("tqbuf", [128, FF], BF16))
    with es2:

        V(lambda e: e.memset(acc_abs[:, :], 0.0))
        vbar()
        cons_tok = None
        for dsh, ntile, cols, agin, sidx in ((w1_sh, 2, FF, ag_w1_in, 0),
                                             (w2_sh, 2, FF, ag_w2_in, 1),
                                             (w3_sh, 8, C, ag_w3_in, 2)):
            for tix in range(ntile):
                tl = dma("sync", prep_in[:, 0:cols],
                         dsh[tix * 128:(tix + 1) * 128, 0:cols], wait=cons_tok)
                ctx.wait("scalar", tl)
                S(lambda e, c=cols: e.activation(out=prep_bf[:, 0:c], in_=prep_in[:, 0:c],
                                                 func=AF.Sign))
                s_sg = sdone()
                ctx.wait("vector", tl)
                V(lambda e, c=cols: e.scalar_tensor_tensor(
                    out=junk_bf[:, 0:c], in0=prep_in[:, 0:c], scalar=-1.0,
                    in1=prep_in[:, 0:c], op0=ALU.mult, op1=ALU.max,
                    accum_out=st_t1[:, 0:1]))
                vbar()
                V(lambda e, si=sidx: e.tensor_tensor(
                    out=st_t2[:, 0:1], in0=acc_abs[:, si:si + 1],
                    in1=st_t1[:, 0:1], op=ALU.add))
                vbar()
                V(lambda e, si=sidx: e.tensor_scalar(
                    out=acc_abs[:, si:si + 1], in0=st_t2[:, 0:1], scalar1=1.0,
                    scalar2=None, op0=ALU.mult))
                v_acc = vbar()
                st_tok = dma("sync", agin.ap()[tix * 128:(tix + 1) * 128, 0:cols],
                             prep_bf[:, 0:cols], sem="st", wait=s_sg)
                cons_tok = st_tok
        v_abs = ctx.tok("vs")
        ctx.wait("tensor", v_abs)
        ctx.wait("tensor", v_ones)
        t_red = ctx.inc("ts", 1)
        ctx.on("tensor", lambda e: e.matmul(pp0[0:1, 0:4], ones128_f[:, :], acc_abs[:, :],
                                            start=True, stop=True).then_inc(ctx.sems["ts"], 1))
        ctx.wait("vector", t_red)
        V(lambda e: e.tensor_copy(acc_row[:, :], pp0[0:1, 0:4]))
        vbar()
        V(lambda e: e.tensor_scalar(out=acc_row[:, :], in0=acc_row[:, :],
                                    scalar1=1.0 / (FF * C), scalar2=None, op0=ALU.mult))
        v_scrow = vdone()
        dma("sync", ar_in.ap(), acc_row[:, :], sem="st", wait=v_scrow)

        tp_begin(ptp_a, ptp_b)

        def ternary_tile(cols, aginT, tix, cons_tok):
            ng = cols // 64
            sl = (slice(None), slice(0, cols))
            gl = (slice(None), slice(0, ng))
            ctx.wait("scalar", cons_tok)
            S(lambda e, c=cols: e.copy(out=prep_bf[:, 0:c], in_=prep_in[:, 0:c]))
            s_cp = sdone()
            ctx.wait("vector", s_cp)
            g3 = prep_bf[sl].rearrange("p (g e) -> p g e", e=64)
            V(lambda e, g3=g3, gl=gl: e.tensor_reduce(
                out=prep_sc[gl], in_=g3, axis=mybir.AxisListType.X,
                op=ALU.add, apply_absolute_value=True))
            vbar()
            V(lambda e, gl=gl: e.tensor_scalar(out=prep_rc[gl], in0=prep_sc[gl],
                                               scalar1=1.0 / 64, scalar2=1e-8,
                                               op0=ALU.mult, op1=ALU.max))
            vbar()
            V(lambda e, gl=gl: e.tensor_scalar(out=prep_sc[gl], in0=prep_rc[gl],
                                               scalar1=1.0, scalar2=None, op0=ALU.mult))
            vbar()
            V(lambda e, gl=gl: e.tensor_scalar(out=prep_scb[gl], in0=prep_sc[gl],
                                               scalar1=1.0, scalar2=None, op0=ALU.mult))
            vbar()
            V(lambda e, gl=gl: e.reciprocal(prep_rc[gl], prep_scb[gl]))
            vbar()
            rcb = prep_rc[gl].unsqueeze(2).broadcast_to([128, ng, 64])
            scb3 = prep_scb[gl].unsqueeze(2).broadcast_to([128, ng, 64])
            qb3 = junk_bf[sl].rearrange("p (g e) -> p g e", e=64)
            V(lambda e, g3=g3, rcb=rcb, qb3=qb3: e.tensor_tensor(
                out=qb3, in0=g3, in1=rcb, op=ALU.mult))
            vbar()
            V(lambda e, sl=sl: e.tensor_scalar(out=prep_in[sl], in0=junk_bf[sl],
                                               scalar1=MAGF, scalar2=MAGF,
                                               op0=ALU.add, op1=ALU.subtract))
            vbar()
            V(lambda e, sl=sl: e.tensor_scalar(out=prep_q[sl], in0=prep_in[sl],
                                               scalar1=-1.0, scalar2=1.0,
                                               op0=ALU.max, op1=ALU.min))
            vbar()
            q3 = prep_q[sl].rearrange("p (g e) -> p g e", e=64)
            o3 = prep_bf[sl].rearrange("p (g e) -> p g e", e=64)
            V(lambda e, q3=q3, o3=o3, scb3=scb3: e.tensor_tensor(
                out=o3, in0=q3, in1=scb3, op=ALU.mult))
            v_q = vbar()
            for j in range(cols // 128):
                ctx.wait("tensor", v_q)
                tp_chunk(prep_bf[:, j * 128:(j + 1) * 128], tqbuf[:, j * 128:(j + 1) * 128])
            e1, e2 = tp_flush()
            outap = aginT.ap()[0:cols, tix * 128:(tix + 1) * 128]
            st_tok = dma3("sync", tqbuf[:, 0:cols], outap, 128, sem="st", wait=e1,
                          to_dram=True)
            ctx.wait("sync", e2)
            return ctx.tok("ts"), st_tok

        tern_cons = cons_tok
        tq_cons = None
        ctx.wait("sync", ctx.tok("vs"))
        ctx.wait("sync", ctx.tok("ss"))
        for tix in range(8):
            tl = dma("sync", prep_in[:, 0:C], fc_sh[tix * 128:(tix + 1) * 128, 0:C],
                     wait=tq_cons)
            ctx.wait("scalar", tl)
            ctx.wait("vector", tl)
            tern_cons, tq_cons = ternary_tile(C, ag_fc_in, tix, tern_cons)
        for tix in range(2):
            tl = dma("sync", prep_in[:, 0:FF], pj_sh[tix * 128:(tix + 1) * 128, 0:FF],
                     wait=tq_cons)
            ctx.wait("scalar", tl)
            ctx.wait("vector", tl)
            tern_cons, tq_cons = ternary_tile(FF, ag_pj_in, tix, tern_cons)

        ctx.wait("gpsimd", ctx.tok("st"))
        cc_h = ctx.sems["cc"]
        ctx.inc("cc", 1)
        ctx.on("gpsimd", lambda e: e.collective_compute(
            "AllReduce", ALU.add, replica_groups=[list(range(NC))],
            ins=[ar_in.ap()], outs=[ar_out.ap()]).then_inc(cc_h, 1))
        for a_, b_ in ((ag_w1_in, ag_w1_out), (ag_w2_in, ag_w2_out),
                       (ag_w3_in, ag_w3_out), (ag_fc_in, ag_fc_out),
                       (ag_pj_in, ag_pj_out)):
            ctx.inc("cc", 1)
            ctx.on("gpsimd", lambda e, a=a_, b=b_: e.collective_compute(
                "AllGather", ALU.bypass, replica_groups=[list(range(NC))],
                ins=[a.ap()], outs=[b.ap()]).then_inc(cc_h, 1))
        cc_w = ctx.tok("cc")

        t_sc = dma("gpsimd", acc_row[:, :], ar_out.ap(), sem="gd", wait=cc_w)
        ctx.wait("tensor", t_sc)
        t_scb = ctx.inc("ts", 1)
        ctx.on("tensor", lambda e: e.matmul(pp0[:, 0:4], ones_row_f[0:1, :],
                                            acc_row[0:1, 0:4], start=True,
                                            stop=True).then_inc(ctx.sems["ts"], 1))
        ctx.wait("vector", t_scb)
        V(lambda e: e.tensor_copy(sscale[:, :], pp0[:, 0:4]))
        v_sscale = vdone()

    phase_barrier()

    def pre_ln_transpose(src_kind, ntiles, dst_cb):
        def src_ap(t):
            return (xh if src_kind == "xh" else xres)[t * 128:(t + 1) * 128, :]
        ctx.wait("sync", ctx.tok("st"))
        cons = None
        for t in range(ntiles):
            tl = dma("sync", xtile[:, :], src_ap(t), wait=cons)
            ctx.wait("vector", tl)
            stats_accum(xtile[:, :], t, C)
            cons = vbar()
        stats_finish(ntiles, EPS, C, st_mu, st_rs)
        for t in range(ntiles):
            tl = dma("sync", xtile[:, :], src_ap(t), wait=cons)
            ctx.wait("vector", tl)
            ctx.wait("vector", ctx.tok("ts"))
            V(lambda e, t=t: e.tensor_scalar(out=xnbf[:, :], in0=xtile[:, :],
                                             scalar1=st_mu[:, t:t + 1],
                                             scalar2=st_rs[:, t:t + 1],
                                             op0=ALU.subtract, op1=ALU.mult))
            v_n = vbar()
            cons = v_n
            for j in range(CT):
                ctx.wait("tensor", v_n)
                dst_cb(t, j)

    def post_block(with_ln1, residual_src, dst):
        ctx.wait("sync", ctx.tok("st"))
        cons = None
        for t in range(NT):
            tl = dma("sync", xtile[:, :], zsp[t * 128:(t + 1) * 128, :], wait=cons)
            ctx.wait("vector", tl)
            stats_accum(xtile[:, :], t, C)
            cons = vbar()
        stats_finish(NT, EPS, C, st_mu, st_rs)
        pass1_cons = cons
        if with_ln1:
            es2 = ExitStack()
            pb_bc = es2.enter_context(nc.psum_tensor("pb_bc", [128, 512], F32))
            lng_bc = es2.enter_context(nc.sbuf_tensor("lng_bc", [128, C], F32))
            lnb_bc = es2.enter_context(nc.sbuf_tensor("lnb_bc", [128, C], F32))
            with es2:
                t_r1 = dma("sync", rowA[:, :], rows_d["lng_row"])
                t_r2 = dma("sync", rowB[:, :], rows_d["lnb_row"])
                bcast_rows(pb_bc, [(t_r1, rowA, lng_bc), (t_r2, rowB, lnb_bc)])
                V(lambda e: e.tensor_copy(st_mu2[:, 0:NT], st_mu[:, 0:NT]))
                V(lambda e: e.tensor_copy(st_rs2[:, 0:NT], st_rs[:, 0:NT]))
                vbar()
                cons = pass1_cons
                for t in range(NT):
                    tl = dma("sync", xtile[:, :], zsp[t * 128:(t + 1) * 128, :], wait=cons)
                    ctx.wait("vector", tl)
                    V(lambda e, t=t: e.tensor_scalar(out=ztmpA[:, :], in0=xtile[:, :],
                                                     scalar1=st_mu2[:, t:t + 1],
                                                     scalar2=st_rs2[:, t:t + 1],
                                                     op0=ALU.subtract, op1=ALU.mult))
                    cons = vbar()
                    V(lambda e: e.tensor_tensor(out=ztmpB[:, :], in0=ztmpA[:, :],
                                                in1=lng_bc[:, :], op=ALU.mult))
                    vbar()
                    V(lambda e: e.tensor_tensor(out=ztmpA[:, :], in0=ztmpB[:, :],
                                                in1=lnb_bc[:, :], op=ALU.add))
                    vbar()
                    stats_accum(ztmpA[:, :], t, C)
                    v_w = vdone()
                    dma("sync", zsp[t * 128:(t + 1) * 128, :], ztmpA[:, :], sem="st",
                        wait=v_w)
                stats_finish(NT, EPS, C, st_mu, st_rs)
                pass1_cons = cons
        res_ap = (lambda t: xh[HALO + t * 128:HALO + (t + 1) * 128, :]) \
            if residual_src == "xh" else (lambda t: xres[t * 128:(t + 1) * 128, :])
        ctx.wait("sync", ctx.tok("st"))
        last_store = None
        consA = pass1_cons
        consB = None
        for t in range(NT):
            tl = dma("sync", xtile[:, :], zsp[t * 128:(t + 1) * 128, :], wait=consA)
            tr = dma("sync", ztmpB[:, :], res_ap(t), wait=consB)
            ctx.wait("vector", tl)
            V(lambda e, t=t: e.tensor_scalar(out=ztmpA[:, :], in0=xtile[:, :],
                                             scalar1=st_mu[:, t:t + 1],
                                             scalar2=st_rs[:, t:t + 1],
                                             op0=ALU.subtract, op1=ALU.mult))
            consA = vbar()
            ctx.wait("vector", tr)
            V(lambda e: e.tensor_tensor(out=xtile[:, :], in0=ztmpA[:, :],
                                        in1=ztmpB[:, :], op=ALU.add))
            consB = vdone()
            consA = consB
            last_store = dma("sync", dst[t * 128:(t + 1) * 128, :], xtile[:, :],
                             sem="st", wait=consB)
        return last_store

    def l0_attn():
        es2 = ExitStack()
        tpa = es2.enter_context(nc.psum_tensor("l0_tpa", [128, 1024], BF16))
        tpb = es2.enter_context(nc.psum_tensor("l0_tpb", [128, 1024], BF16))
        pk = es2.enter_context(nc.psum_tensor("l0_pk", [128, 512], F32))
        pv = es2.enter_context(nc.psum_tensor("l0_pv", [128, 512], F32))
        pk2 = es2.enter_context(nc.psum_tensor("l0_pk2", [128, 512], F32))
        tT = es2.enter_context(nc.sbuf_tensor("l0_tT", [128, CT * 128], BF16))
        mixa = es2.enter_context(nc.sbuf_tensor("l0_mixa", [128, CT * THL], BF16))
        mixb = es2.enter_context(nc.sbuf_tensor("l0_mixb", [128, CT * THL], BF16))
        xnrow = es2.enter_context(nc.sbuf_tensor("l0_xnrow", [128, THL], BF16))
        xxrow = es2.enter_context(nc.sbuf_tensor("l0_xxrow", [128, THL], BF16))
        wcol = es2.enter_context(nc.sbuf_tensor("l0_wcol", [128, 2, CT * 128], BF16))
        wcol2 = es2.enter_context(nc.sbuf_tensor("l0_wcol2", [128, 2, CT * 128], BF16))
        wkvt = es2.enter_context(nc.sbuf_tensor("l0_wkv", [128, TOK], BF16))
        with es2:

            tp_begin(tpa, tpb)

            def dst_cb(t, j):
                tp_chunk(xnbf[:, j * 128:(j + 1) * 128], tT[:, j * 128:(j + 1) * 128])
                if j == CT - 1:
                    e1, e2 = tp_flush()
                    ctx.wait("sync", e2)
                    outap = xnT_sp[0:C, t * 128:(t + 1) * 128]
                    sp = dma3("sync", tT[:, :], outap, 128, sem="st", wait=e1,
                              to_dram=True)
                    ctx.wait("scalar", sp)
            pre_ln_transpose("xh", NTH, dst_cb)
            if DEBUG_STOP == "l0a:ln":
                return

            def build_mixes(mix_specs):
                ctx.wait("sync", ctx.tok("st"))
                cons = ctx.tok("vs")
                for j in range(CT):
                    tl = dma("sync", xnrow[:, :], xnT_sp[j * 128:(j + 1) * 128, 0:THL],
                             wait=cons)
                    ctx.wait("vector", tl)
                    V(lambda e: e.memset(xxrow[:, 0:1], 0.0))
                    V(lambda e: e.tensor_tensor(out=xxrow[:, 1:THL], in0=xnrow[:, 0:THL - 1],
                                                in1=xnrow[:, 1:THL], op=ALU.subtract))
                    vbar()
                    for tmv_, dstb in mix_specs:
                        V(lambda e, j=j, tm=tmv_, d=dstb: e.scalar_tensor_tensor(
                            out=d[:, j * THL:(j + 1) * THL], in0=xxrow[:, :],
                            scalar=tm[:, j:j + 1], in1=xnrow[:, :],
                            op0=ALU.mult, op1=ALU.add))
                    cons = vbar()

            def load_wcol(buf, half, wname, o, guard):
                src = wT_d[wname][:, o * 128:(o + 1) * 128]
                return dma3("gpsimd", buf[:, half, :], src, 128, sem="gd", wait=guard)

            esA = ExitStack()
            ekb = esA.enter_context(nc.sbuf_tensor("l0_ek", [128, THL], F32))
            pb_ = esA.enter_context(nc.sbuf_tensor("l0_p", [128, THL], F32))
            wtb = esA.enter_context(nc.sbuf_tensor("l0_wt", [128, THL], F32))
            Ab = esA.enter_context(nc.sbuf_tensor("l0_A", [128, THL + 1], F32))
            Bb = esA.enter_context(nc.sbuf_tensor("l0_B", [128, THL + 1], F32))
            build_mixes([(v16["tmk"], mixa), (v16["tmv"], mixb)])
            if DEBUG_STOP in ("l0a:mix", "l0a:xnt"):
                esA.close()
                return
            tchunks = [(0, 512), (512, 512), (1024, 256)]
            guards = [None, None]
            wk_tok = [load_wcol(wcol, 0, "wk0", 0, None), None]
            wv_tok = [load_wcol(wcol2, 0, "wv0", 0, None), None]
            ev_prev = None
            for o in range(CT):
                h = o % 2
                if o + 1 < CT:
                    wk_tok[1 - h] = load_wcol(wcol, 1 - h, "wk0", o + 1, guards[1 - h])
                    wv_tok[1 - h] = load_wcol(wcol2, 1 - h, "wv0", o + 1, guards[1 - h])
                ctx.wait("tensor", wk_tok[h])
                ctx.wait("tensor", wv_tok[h])
                ctx.wait("tensor", ctx.tok("vs"))
                for (c0, w) in tchunks:
                    ctx.wait("tensor", ev_prev)
                    for j in range(CT):
                        mm(pk[:, 0:w], wcol[:, h, j * 128:(j + 1) * 128],
                           mixa[:, j * THL + c0:j * THL + c0 + w],
                           start=(j == 0), stop=(j == CT - 1))
                    t_k = tmark()
                    for j in range(CT):
                        mm(pv[:, 0:w], wcol2[:, h, j * 128:(j + 1) * 128],
                           mixb[:, j * THL + c0:j * THL + c0 + w],
                           start=(j == 0), stop=(j == CT - 1))
                    t_v = tmark()
                    if DEBUG_STOP == "l0a:A00":
                        esA.close()
                        return
                    ctx.wait("scalar", t_k)
                    if DEBUG_STOP == "l0a:A01s":
                        S(lambda e, c0=c0, w=w: e.copy(out=ekb[:, c0:c0 + w],
                                                       in_=hmask[:, 0:w]))
                    elif DEBUG_STOP == "l0a:A01c":
                        S(lambda e, c0=c0, w=w: e.copy(out=ekb[:, c0:c0 + w],
                                                       in_=pk[:, 0:w]))
                    else:
                        S(lambda e, c0=c0, w=w: e.activation(out=ekb[:, c0:c0 + w],
                                                             in_=pk[:, 0:w], func=AF.Exp))
                    s_ek = sdone()
                    if DEBUG_STOP in ("l0a:A01", "l0a:A01c", "l0a:A01s"):
                        esA.close()
                        return
                    ctx.wait("vector", s_ek)
                    ctx.wait("vector", t_v)
                    V(lambda e, c0=c0, w=w: e.tensor_tensor(
                        out=ekb[:, c0:c0 + w], in0=ekb[:, c0:c0 + w],
                        in1=hmask[:, c0:c0 + w], op=ALU.mult))
                    vbar()
                    if DEBUG_STOP == "l0a:A02":
                        esA.close()
                        return
                    V(lambda e, c0=c0, w=w: e.tensor_tensor(
                        out=pb_[:, c0:c0 + w], in0=ekb[:, c0:c0 + w],
                        in1=pv[:, 0:w], op=ALU.mult))
                    ev_prev = vbar()
                guards[h] = ev_prev
                V(lambda e, o=o: e.tensor_scalar(out=wtb[:, :], in0=ekb[:, :], scalar1=0.0,
                                                 scalar2=wdec[:, o:o + 1],
                                                 op0=ALU.mult, op1=ALU.add))
                V(lambda e: e.memset(Ab[:, 0:1], 0.0))
                V(lambda e: e.memset(Bb[:, 0:1], 0.0))
                vbar()
                V(lambda e: e.tensor_tensor_scan(out=Ab[:, 1:THL + 1], data0=wtb[:, :],
                                                 data1=pb_[:, :], initial=0.0,
                                                 op0=ALU.mult, op1=ALU.add))
                V(lambda e: e.tensor_tensor_scan(out=Bb[:, 1:THL + 1], data0=wtb[:, :],
                                                 data1=ekb[:, :], initial=0.0,
                                                 op0=ALU.mult, op1=ALU.add))
                vbar()
                V(lambda e, o=o: e.tensor_scalar(out=wtb[:, HALO:THL],
                                                 in0=ekb[:, HALO:THL],
                                                 scalar1=eu[:, o:o + 1], scalar2=None,
                                                 op0=ALU.mult))
                vbar()
                V(lambda e: e.tensor_tensor(out=ekb[:, HALO:THL], in0=wtb[:, HALO:THL],
                                            in1=Bb[:, HALO:THL], op=ALU.add))
                vbar()
                V(lambda e: e.reciprocal(wtb[:, HALO:THL], ekb[:, HALO:THL]))
                vbar()
                V(lambda e, o=o: e.tensor_scalar(out=ekb[:, HALO:THL],
                                                 in0=pb_[:, HALO:THL],
                                                 scalar1=eu[:, o:o + 1], scalar2=None,
                                                 op0=ALU.mult))
                vbar()
                V(lambda e: e.tensor_tensor(out=pb_[:, HALO:THL], in0=ekb[:, HALO:THL],
                                            in1=Ab[:, HALO:THL], op=ALU.add))
                vbar()
                ctx.wait("vector", ctx.tok("st"))
                V(lambda e: e.tensor_tensor(out=wkvt[:, :], in0=pb_[:, HALO:THL],
                                            in1=wtb[:, HALO:THL], op=ALU.mult))
                v_wkv = vdone()
                dma("sync", wkv_sp[o * 128:(o + 1) * 128, :], wkvt[:, :], sem="st",
                    wait=v_wkv)
                if DEBUG_STOP == "l0a:A1":
                    esA.close()
                    return

            if DEBUG_STOP in ("l0a:A", "l0a:wkv"):
                esA.close()
                return
            esA.close()
            esB = ExitStack()
            srb = esB.enter_context(nc.sbuf_tensor("l0_sr", [128, CT * TOK], BF16))
            sgtmp = xnbf
            build_mixes([(v16["tmr"], mixa)])
            if DEBUG_STOP == "l0a:B1m":
                ctx.wait("sync", ctx.tok("vs"))
                dma("sync", out_d[0:640, :].bitcast(BF16), mixa[:, :], sem="st")
                esB.close()
                return
            guards = [None, None]
            wk_tok = [load_wcol(wcol, 0, "wr", 0, ctx.tok("ts")), None]
            ev_prev = None
            for o in range(CT):
                h = o % 2
                if o + 1 < CT:
                    wk_tok[1 - h] = load_wcol(wcol, 1 - h, "wr", o + 1, guards[1 - h])
                ctx.wait("tensor", wk_tok[h])
                ctx.wait("tensor", ctx.tok("vs"))
                for ci, (c0, w) in enumerate(((HALO, 512), (HALO + 512, 512))):
                    ctx.wait("tensor", ev_prev)
                    pps = pk if ci == 0 else pk2
                    for j in range(CT):
                        mm(pps[:, 0:w], wcol[:, h, j * 128:(j + 1) * 128],
                           mixa[:, j * THL + c0:j * THL + c0 + w],
                           start=(j == 0), stop=(j == CT - 1))
                    t_k = tmark()
                    ctx.wait("scalar", t_k)
                    ctx.wait("scalar", ev_prev)
                    S(lambda e, p=pps, w=w: e.activation(out=ztmpA[:, 0:w], in_=p[:, 0:w],
                                                         func=AF.Tanh, scale=0.5))
                    s_t = sdone()
                    ctx.wait("vector", s_t)
                    V(lambda e, o=o, ci=ci, w=w: e.tensor_scalar(
                        out=srb[:, o * TOK + ci * 512:o * TOK + ci * 512 + w],
                        in0=ztmpA[:, 0:w], scalar1=0.5, scalar2=0.5,
                        op0=ALU.mult, op1=ALU.add))
                    ev_prev = vbar()
                guards[h] = ev_prev

            if DEBUG_STOP == "l0a:B1d":
                ctx.wait("sync", ctx.tok("vs"))
                dma("sync", out_d[0:512, :].bitcast(BF16), srb[:, 0:CT * TOK], sem="st")
                esB.close()
                return
            build_mixes([(v16["tmg"], mixb)])
            attnT = mixa
            ctx.wait("sync", ctx.tok("st"))
            guards = [None, None]
            wk_tok = [load_wcol(wcol, 0, "wg", 0, ctx.tok("ts")), None]
            ev_prev = None
            t_wk_prev = None
            for o in range(CT):
                h = o % 2
                if o + 1 < CT:
                    wk_tok[1 - h] = load_wcol(wcol, 1 - h, "wg", o + 1, guards[1 - h])
                ctx.wait("tensor", wk_tok[h])
                ctx.wait("tensor", ctx.tok("vs"))
                t_wk = dma("sync", wkvt[:, :], wkv_sp[o * 128:(o + 1) * 128, :],
                           wait=ev_prev)
                for ci, (c0, w) in enumerate(((HALO, 512), (HALO + 512, 512))):
                    ctx.wait("tensor", ev_prev)
                    pps = pk if ci == 0 else pk2
                    for j in range(CT):
                        mm(pps[:, 0:w], wcol[:, h, j * 128:(j + 1) * 128],
                           mixb[:, j * THL + c0:j * THL + c0 + w],
                           start=(j == 0), stop=(j == CT - 1))
                    t_k = tmark()
                    ctx.wait("scalar", t_k)
                    ctx.wait("scalar", ev_prev)
                    S(lambda e, p=pps, w=w: e.activation(out=ztmpA[:, 0:w], in_=p[:, 0:w],
                                                         func=AF.Tanh, scale=0.5))
                    s_t = sdone()
                    ctx.wait("vector", s_t)
                    ctx.wait("vector", t_wk)
                    V(lambda e, w=w: e.tensor_scalar(out=sgtmp[:, 0:w], in0=ztmpA[:, 0:w],
                                                     scalar1=0.5, scalar2=0.5,
                                                     op0=ALU.mult, op1=ALU.add))
                    vbar()
                    V(lambda e, o=o, ci=ci, w=w: e.tensor_tensor(
                        out=ztmpB[:, 0:w],
                        in0=srb[:, o * TOK + ci * 512:o * TOK + ci * 512 + w],
                        in1=wkvt[:, ci * 512:ci * 512 + w], op=ALU.mult))
                    vbar()
                    if DEBUG_STOP == "l0a:attg":
                        V(lambda e, o=o, ci=ci, w=w: e.tensor_copy(
                            attnT[:, o * TOK + ci * 512:o * TOK + ci * 512 + w],
                            sgtmp[:, 0:w]))
                    elif DEBUG_STOP == "l0a:attb":
                        V(lambda e, o=o, ci=ci, w=w: e.tensor_copy(
                            attnT[:, o * TOK + ci * 512:o * TOK + ci * 512 + w],
                            ztmpB[:, 0:w]))
                    elif DEBUG_STOP == "l0a:attw":
                        V(lambda e, o=o, ci=ci, w=w: e.tensor_copy(
                            attnT[:, o * TOK + ci * 512:o * TOK + ci * 512 + w],
                            wkvt[:, ci * 512:ci * 512 + w]))
                    elif DEBUG_STOP == "l0a:attr":
                        V(lambda e, o=o, ci=ci, w=w: e.tensor_copy(
                            attnT[:, o * TOK + ci * 512:o * TOK + ci * 512 + w],
                            srb[:, o * TOK + ci * 512:o * TOK + ci * 512 + w]))
                    else:
                        V(lambda e, w=w: e.tensor_tensor(
                            out=ztmpA[:, 0:w], in0=ztmpB[:, 0:w], in1=sgtmp[:, 0:w],
                            op=ALU.mult))
                        vbar()
                        V(lambda e, o=o, ci=ci, w=w: e.tensor_copy(
                            attnT[:, o * TOK + ci * 512:o * TOK + ci * 512 + w],
                            ztmpA[:, 0:w]))
                    ev_prev = vbar()
                guards[h] = ev_prev
            v_attn = ctx.tok("vs")

            if DEBUG_STOP in ("l0a:B", "l0a:att", "l0a:attg", "l0a:attb", "l0a:attw", "l0a:attr"):
                if DEBUG_STOP in ("l0a:att", "l0a:attg", "l0a:attb", "l0a:attw", "l0a:attr"):
                    ctx.wait("sync", ctx.tok("vs"))
                    dma("sync", out_d[0:512, :].bitcast(BF16), attnT[:, 0:CT * TOK], sem="st")
                esB.close()
                return
            esB.close()
            with nc.sbuf_tensor("l0_wob", [128, 2, CT * 512], BF16) as wob:
                def load_woblk(half, ocs, guard):
                    src = wT_d["wo0"][:, ocs * 512:(ocs + 1) * 512]
                    return dma3("gpsimd", wob[:, half, :], src, 512, sem="gd", wait=guard)
                wo_guard = [None, None]
                wo_tok = [load_woblk(0, 0, ctx.tok("ts")), None]
                ev_prev = None
                for ocs in range(4):
                    h = ocs % 2
                    if ocs + 1 < 4:
                        wo_tok[1 - h] = load_woblk(1 - h, ocs + 1, wo_guard[1 - h])
                    ctx.wait("tensor", wo_tok[h])
                    ctx.wait("tensor", v_attn)
                    for t in range(NT):
                        ctx.wait("tensor", ev_prev)
                        for j in range(CT):
                            mm(pk[:, :], attnT[:, j * TOK + t * 128:j * TOK + (t + 1) * 128],
                               wob[:, h, j * 512:(j + 1) * 512],
                               start=(j == 0), stop=(j == CT - 1))
                        t_k = tmark()
                        ctx.wait("vector", t_k)
                        ctx.wait("vector", ctx.tok("st"))
                        V(lambda e: e.tensor_copy(ztmpA[:, 0:512], pk[:, :]))
                        ev_prev = vdone()
                        dma("sync", zsp[t * 128:(t + 1) * 128, ocs * 512:(ocs + 1) * 512],
                            ztmpA[:, 0:512], sem="st", wait=ev_prev)
                    wo_guard[h] = tmark()
        phase_barrier()
        post_block(True, "xh", xres)
        phase_barrier()

    def l0_ffn():
        es2 = ExitStack()
        tpa = es2.enter_context(nc.psum_tensor("f0_tpa", [128, 1024], BF16))
        tpb = es2.enter_context(nc.psum_tensor("f0_tpb", [128, 1024], BF16))
        p1 = es2.enter_context(nc.psum_tensor("f0_p1", [128, 512], F32))
        p2 = es2.enter_context(nc.psum_tensor("f0_p2", [128, 512], F32))
        xnT = es2.enter_context(nc.sbuf_tensor("f0_xnT", [128, CT * TOK], BF16))
        fcw = es2.enter_context(nc.sbuf_tensor("f0_fcw", [128, 2, CT * 512], BF16))
        htok = es2.enter_context(nc.sbuf_tensor("f0_htok", [128, FF], BF16))
        hrel = es2.enter_context(nc.sbuf_tensor("f0_hrel", [128, 512], F32))
        hnT = es2.enter_context(nc.sbuf_tensor("f0_hnT", [128, FT * 256], BF16))
        pjw = es2.enter_context(nc.sbuf_tensor("f0_pjw", [128, 1, 32 * 256], BF16))
        with es2:

            tp_begin(tpa, tpb)

            def dst_cb(t, j):
                tp_chunk(xnbf[:, j * 128:(j + 1) * 128],
                         xnT[:, j * TOK + t * 128:j * TOK + (t + 1) * 128])
            pre_ln_transpose("xres", NT, dst_cb)
            ctx.wait("sync", ctx.tok("cc"))

            def load_fc(half, ocs, guard):
                r = (ocs * 512) // 1024
                coff = (ocs * 512) % 1024
                src = ag_fc_out.ap()[r * C:(r + 1) * C, coff:coff + 512]
                return dma3("sync", fcw[:, half, :], src, 512, wait=guard)

            def load_pj(half, oc256, part, guard):
                src = ag_pj_out.ap()[oc256 * FF + part * 4096:oc256 * FF + (part + 1) * 4096,
                                     0:256]
                return dma3("sync", pjw[:, half, :], src, 256, wait=guard)

            hnT_guard = None
            fc_guard = [None, None]
            for tg in range(4):
                tts = [2 * tg, 2 * tg + 1]
                for local, t in enumerate(tts):
                    ev_prev = None
                    fc_tok = [None, None]
                    fc_tok[0] = load_fc(0, 0, fc_guard[0])
                    for ocs in range(CT):
                        hh = ocs % 2
                        if ocs + 1 < CT:
                            fc_tok[1 - hh] = load_fc(1 - hh, ocs + 1, fc_guard[1 - hh])
                        ctx.wait("tensor", fc_tok[hh])
                        ctx.wait("tensor", ev_prev)
                        for j in range(CT):
                            mm(p1[:, :], xnT[:, j * TOK + t * 128:j * TOK + (t + 1) * 128],
                               fcw[:, hh, j * 512:(j + 1) * 512],
                               start=(j == 0), stop=(j == CT - 1))
                        t_k = tmark()
                        fc_guard[hh] = t_k
                        ctx.wait("scalar", t_k)
                        if ocs == 0:
                            ctx.wait("scalar", hnT_guard)
                            ctx.wait("scalar", ctx.tok("ts"))
                        S(lambda e: e.activation(out=hrel[:, :], in_=p1[:, :], func=AF.Relu))
                        s_r = sdone()
                        ctx.wait("scalar", s_r)
                        S(lambda e, ocs=ocs: e.activation(
                            out=htok[:, ocs * 512:(ocs + 1) * 512], in_=hrel[:, :],
                            func=AF.Square))
                        ev_prev = sdone()
                    ctx.wait("vector", ev_prev)
                    V(lambda e, c=t: e.scalar_tensor_tensor(
                        out=junk_bf[:, 0:FF], in0=htok[:, :], scalar=1.0, in1=htok[:, :],
                        op0=ALU.mult, op1=ALU.mult, accum_out=st_s2[:, c:c + 1]))
                    vbar()
                    for j in range(FT):
                        ctx.wait("tensor", ev_prev)
                        tp_chunk(htok[:, j * 128:(j + 1) * 128],
                                 hnT[:, j * 256 + local * 128:j * 256 + (local + 1) * 128])
                nsl = (slice(None), slice(2 * tg, 2 * tg + 2))
                V(lambda e, nsl=nsl: e.tensor_scalar(out=st_var[nsl], in0=st_s2[nsl],
                                                     scalar1=1.0 / FF, scalar2=RMS_EPS,
                                                     op0=ALU.mult, op1=ALU.add))
                vbar()
                quake_rsqrt(2, st_var[nsl], st_rs[nsl])
                e1, e2 = tp_flush()
                pj_guard = [None]
                ev2 = None
                for oc256 in range(8):
                    for part in range(2):
                        hh = 0
                        pj_tok = load_pj(0, oc256, part, pj_guard[0])
                        ctx.wait("tensor", pj_tok)
                        ctx.wait("tensor", e1)
                        ctx.wait("tensor", e2)
                        for local, t in enumerate(tts):
                            pp = p1 if local == 0 else p2
                            if part == 0:
                                ctx.wait("tensor", ev2)
                            for jj in range(32):
                                j = part * 32 + jj
                                mm(pp[:, 0:256],
                                   hnT[:, j * 256 + local * 128:j * 256 + (local + 1) * 128],
                                   pjw[:, hh, jj * 256:(jj + 1) * 256],
                                   start=(j == 0), stop=(j == FT - 1))
                            if part == 1:
                                t_k = tmark()
                                ctx.wait("vector", t_k)
                                ctx.wait("vector", ctx.tok("st"))
                                V(lambda e, t=t, pp=pp: e.tensor_scalar(
                                    out=ztmpA[:, 0:256], in0=pp[:, 0:256],
                                    scalar1=st_rs[:, t:t + 1], scalar2=None, op0=ALU.mult))
                                ev2 = vdone()
                                dma("sync",
                                    zsp[t * 128:(t + 1) * 128, oc256 * 256:(oc256 + 1) * 256],
                                    ztmpA[:, 0:256], sem="st", wait=ev2)
                        pj_guard[0] = tmark()
                hnT_guard = tmark()
        phase_barrier()
        post_block(False, "xres", xres)
        phase_barrier()

    def l1_attn():
        es2 = ExitStack()
        xnT = es2.enter_context(nc.sbuf_tensor("a1_xnT", [128, CT * TOK], BF16))
        qT = es2.enter_context(nc.sbuf_tensor("a1_qT", [128, CT * TOK], BF16))
        with es2:

            es2 = ExitStack()
            tpa = es2.enter_context(nc.psum_tensor("a1_tpa", [128, 1024], BF16))
            tpb = es2.enter_context(nc.psum_tensor("a1_tpb", [128, 1024], BF16))
            p1 = es2.enter_context(nc.psum_tensor("a1_p1", [128, 512], F32))
            p2 = es2.enter_context(nc.psum_tensor("a1_p2", [128, 512], F32))
            pbc = es2.enter_context(nc.psum_tensor("a1_pbc", [128, 512], F32))
            wcolb = es2.enter_context(nc.sbuf_tensor("a1_wcolb", [128, 1, CT * 512], BF16))
            qtok = es2.enter_context(nc.sbuf_tensor("a1_qtok", [128, C], F32))
            q2 = es2.enter_context(nc.sbuf_tensor("a1_q2", [128, C], BF16))
            gbc = es2.enter_context(nc.sbuf_tensor("a1_gbc", [128, C], F32))
            msb = es2.enter_context(nc.sbuf_tensor("a1_ms", [128, 16], F32))
            rsg = es2.enter_context(nc.sbuf_tensor("a1_rsg", [128, 16], F32))
            kTb = es2.enter_context(nc.sbuf_tensor("a1_kT", [128, CT * TOK], BF16))
            with es2:

                tp_begin(tpa, tpb)

                def dst_cb(t, j):
                    tp_chunk(xnbf[:, j * 128:(j + 1) * 128],
                             xnT[:, j * TOK + t * 128:j * TOK + (t + 1) * 128])
                pre_ln_transpose("xres", NT, dst_cb)

                def proj_weight(wname, post_cb):
                    w_guard = [ctx.tok("ts")]

                    def load_w(ocs, guard):
                        src = wT_d[wname][:, ocs * 512:(ocs + 1) * 512]
                        return dma3("gpsimd", wcolb[:, 0, :], src, 512, sem="gd",
                                    wait=guard)
                    ev = None
                    idx = 0
                    for t in range(NT):
                        for ocs in range(4):
                            wtok = load_w(ocs, w_guard[0])
                            ctx.wait("tensor", wtok)
                            ctx.wait("tensor", ev)
                            pp = p1 if idx % 2 == 0 else p2
                            for j in range(CT):
                                mm(pp[:, :], xnT[:, j * TOK + t * 128:j * TOK + (t + 1) * 128],
                                   wcolb[:, 0, j * 512:(j + 1) * 512],
                                   start=(j == 0), stop=(j == CT - 1))
                            t_k = tmark()
                            w_guard[0] = t_k
                            ctx.wait("vector", t_k)
                            V(lambda e, ocs=ocs, pp=pp: e.tensor_copy(
                                qtok[:, ocs * 512:(ocs + 1) * 512], pp[:, :]))
                            ev = vdone()
                            idx += 1
                        post_cb(t)

                def qk_post(t, dst_sb):
                    vbar()
                    ctx.wait("scalar", ctx.tok("vs"))
                    S(lambda e: e.activation(out=junk_bf[:, 0:C], in_=qtok[:, :],
                                             func=AF.Square))
                    s_q2 = sdone()
                    ctx.wait("vector", s_q2)
                    g3 = junk_bf[:, 0:C].rearrange("p (h d) -> p h d", d=128)
                    V(lambda e, g3=g3: e.tensor_reduce(out=msb[:, :], in_=g3,
                                                       axis=mybir.AxisListType.X, op=ALU.add))
                    vbar()
                    V(lambda e: e.tensor_scalar(out=st_t2[:, 0:16], in0=msb[:, :],
                                                scalar1=1.0 / 128, scalar2=EPS,
                                                op0=ALU.mult, op1=ALU.add))
                    vbar()
                    V(lambda e: e.tensor_scalar(out=msb[:, :], in0=st_t2[:, 0:16],
                                                scalar1=1.0, scalar2=None, op0=ALU.mult))
                    vbar()
                    quake_rsqrt(16, msb[:, :], rsg[:, :])
                    for hh2 in range(16):
                        V(lambda e, hh2=hh2: e.tensor_scalar(
                            out=ztmpA[:, hh2 * 128:(hh2 + 1) * 128],
                            in0=qtok[:, hh2 * 128:(hh2 + 1) * 128],
                            scalar1=rsg[:, hh2:hh2 + 1], scalar2=None, op0=ALU.mult))
                    vbar()
                    ctx.wait("vector", ctx.tok("ts"))
                    V(lambda e: e.tensor_tensor(out=xnbf[:, :], in0=ztmpA[:, :],
                                                in1=gbc[:, :], op=ALU.mult))
                    v_n = vbar()
                    for j in range(CT):
                        ctx.wait("tensor", v_n)
                        tp_chunk(xnbf[:, j * 128:(j + 1) * 128],
                                 dst_sb[:, j * TOK + t * 128:j * TOK + (t + 1) * 128])

                t_r1 = dma("sync", rowA[:, :], rows_d["gq_row"])
                ctx.wait("vector", t_r1)
                V(lambda e: e.tensor_scalar(out=rowA[0:1, :], in0=rowA[0:1, :],
                                            scalar1=INV_SQRT_HD, scalar2=None, op0=ALU.mult))
                v_g = vbar()
                bcast_rows(pbc, [(v_g, rowA, gbc)])
                proj_weight("wq", lambda t: qk_post(t, qT))

                t_r2 = dma("sync", rowB[:, :], rows_d["gk_row"])
                bcast_rows(pbc, [(t_r2, rowB, gbc)])
                proj_weight("wk1", lambda t: qk_post(t, kTb))
                e1, e2 = tp_flush()
                ctx.wait("sync", e1)
                ctx.wait("sync", e2)
                for hh in range(16):
                    dma("sync", kv_in.ap()[hh * 128:(hh + 1) * 128, :],
                        kTb[:, hh * TOK:(hh + 1) * TOK], sem="st")

                def v_post(t):
                    vbar()
                    ctx.wait("vector", ctx.tok("st"))
                    V(lambda e: e.tensor_copy(xnbf[:, :], qtok[:, :]))
                    v_n = vdone()
                    for hh in range(16):
                        dma("sync", kv_in.ap()[2048 + hh * 128:2048 + (hh + 1) * 128,
                                               t * 128:(t + 1) * 128],
                            xnbf[:, hh * 128:(hh + 1) * 128], sem="st", wait=v_n)
                proj_weight("wv1", v_post)

                ctx.wait("gpsimd", ctx.tok("st"))
                ctx.inc("cc", 1)
                ctx.on("gpsimd", lambda e: e.collective_compute(
                    "AllGather", ALU.bypass,
                    replica_groups=[[0, 1], [2, 3], [4, 5], [6, 7]],
                    ins=[kv_in.ap()], outs=[kv_out.ap()]).then_inc(ctx.sems["cc"], 1))

            es2 = ExitStack()
            sA = es2.enter_context(nc.psum_tensor("a1_sA", [128, 1024], F32))
            sB = es2.enter_context(nc.psum_tensor("a1_sB", [128, 1024], F32))
            av = es2.enter_context(nc.psum_tensor("a1_av", [128, 1024], F32))
            sm = es2.enter_context(nc.psum_tensor("a1_sm", [128, 1024], F32))
            kh = es2.enter_context(nc.sbuf_tensor("a1_kh", [128, 2, TOK], BF16))
            vh = es2.enter_context(nc.sbuf_tensor("a1_vh", [128, 2, TOK], BF16))
            pt = es2.enter_context(nc.sbuf_tensor("a1_pt", [128, 2, TOK], BF16))
            srow = es2.enter_context(nc.sbuf_tensor("a1_srow", [1, TOK], F32))
            rrow = es2.enter_context(nc.sbuf_tensor("a1_rrow", [1, TOK], F32))
            rbc = es2.enter_context(nc.sbuf_tensor("a1_rbc", [128, TOK], F32))
            attnT = es2.enter_context(nc.sbuf_tensor("a1_attnT", [128, CT * TOK], BF16))
            with es2:

                ctx.wait("sync", ctx.tok("cc"))

                def load_kvh(half, hh, blk, kind, guard):
                    base = blk * 4096 + (0 if kind == "k" else 2048) + hh * 128
                    buf = kh if kind == "k" else vh
                    return dma("sync", buf[:, half, :], kv_out.ap()[base:base + 128, :],
                               wait=guard)

                ev_attn = None
                s_exp_prev = None
                for hh in range(16):
                    t_av_last = None
                    k_toks = [load_kvh(0, hh, 0, "k", ev_attn), None]
                    v_toks = [load_kvh(0, hh, 0, "v", ev_attn), None]
                    for kt in range(16):
                        blk = kt // 8
                        ktl = kt % 8
                        half = kt % 2
                        if kt + 1 < 16:
                            g = t_av_last
                            k_toks[1 - half] = load_kvh(1 - half, hh, (kt + 1) // 8, "k", g)
                            v_toks[1 - half] = load_kvh(1 - half, hh, (kt + 1) // 8, "v", g)
                        sc_ps = sA if kt % 2 == 0 else sB
                        ctx.wait("tensor", k_toks[half])
                        if kt >= 2:
                            ctx.wait("tensor", s_exp_prev)
                        for wslc in range(2):
                            mm(sc_ps[:, wslc * 512:(wslc + 1) * 512],
                               kh[:, half, ktl * 128:(ktl + 1) * 128],
                               qT[:, hh * TOK + wslc * 512:hh * TOK + (wslc + 1) * 512],
                               start=True, stop=True)
                        wb_t = wedgeAb if blk == 0 else wedgeBb
                        fm_t = fmAb if blk == 0 else fmBb
                        mm(sc_ps[:, ktl * 128:(ktl + 1) * 128], ident[:, :], wb_t[:, :],
                           start=False, stop=True)
                        for qb in range(ktl):
                            mm(sc_ps[:, qb * 128:(qb + 1) * 128], ident[:, :], fm_t[:, :],
                               start=False, stop=True)
                        t_sc = tmark()
                        ctx.wait("scalar", t_sc)
                        if blk == 0:
                            S(lambda e, sc_ps=sc_ps, half=half: e.activation(
                                out=pt[:, half, :], in_=sc_ps[:, :], func=AF.Exp,
                                bias=bias20[:, 0:1]))
                        else:
                            S(lambda e, sc_ps=sc_ps, half=half: e.activation(
                                out=pt[:, half, :], in_=sc_ps[:, :], func=AF.Exp,
                                bias=pmbB[:, 0:1]))
                        s_exp_prev = sdone()
                        ctx.wait("tensor", s_exp_prev)
                        ctx.wait("tensor", v_toks[half])
                        for wslc in range(2):
                            mm(sm[0:1, wslc * 512:(wslc + 1) * 512], ones_col_b[:, :],
                               pt[:, half, wslc * 512:(wslc + 1) * 512],
                               start=(kt == 0), stop=(kt == 15))
                        for wslc in range(2):
                            mm(av[:, wslc * 512:(wslc + 1) * 512],
                               vh[:, half, ktl * 128:(ktl + 1) * 128],
                               pt[:, half, wslc * 512:(wslc + 1) * 512],
                               start=(kt == 0), stop=(kt == 15))
                        t_av = tmark()
                        t_av_last = t_av
                    ctx.wait("scalar", t_av_last)
                    S(lambda e: e.copy(out=srow[0:1, :], in_=sm[0:1, :]))
                    s_sum = sdone()
                    ctx.wait("vector", s_sum)
                    V(lambda e: e.reciprocal(rrow[0:1, :], srow[0:1, :]))
                    v_rr = vbar()
                    ctx.wait("tensor", v_rr)
                    for wslc in range(2):
                        mm(sm[:, wslc * 512:(wslc + 1) * 512], ones_row_f[0:1, :],
                           rrow[0:1, wslc * 512:(wslc + 1) * 512],
                           start=True, stop=True)
                    t_rb = tmark()
                    ctx.wait("vector", t_rb)
                    V(lambda e: e.tensor_copy(rbc[:, :], sm[:, :]))
                    vbar()
                    V(lambda e, hh=hh: e.tensor_tensor(
                        out=attnT[:, hh * TOK:(hh + 1) * TOK], in0=av[:, :],
                        in1=rbc[:, :], op=ALU.mult))
                    ev_attn = vdone()
                    ctx.wait("tensor", ev_attn)

                with nc.sbuf_tensor("a1_wob", [128, 1, CT * 512], BF16) as wob:
                    def load_woblk(ocs, guard):
                        src = wT_d["wo1"][:, ocs * 512:(ocs + 1) * 512]
                        return dma3("gpsimd", wob[:, 0, :], src, 512, sem="gd", wait=guard)
                    wo_guard = [ctx.tok("ts")]
                    ev_prev = None
                    for ocs in range(4):
                        wo_tok = load_woblk(ocs, wo_guard[0])
                        ctx.wait("tensor", wo_tok)
                        ctx.wait("tensor", ev_attn)
                        for t in range(NT):
                            ctx.wait("tensor", ev_prev)
                            for j in range(CT):
                                mm(sA[:, 0:512],
                                   attnT[:, j * TOK + t * 128:j * TOK + (t + 1) * 128],
                                   wob[:, 0, j * 512:(j + 1) * 512],
                                   start=(j == 0), stop=(j == CT - 1))
                            t_k = tmark()
                            ctx.wait("vector", t_k)
                            ctx.wait("vector", ctx.tok("st"))
                            V(lambda e: e.tensor_copy(ztmpA[:, 0:512], sA[:, 0:512]))
                            ev_prev = vdone()
                            dma("sync", zsp[t * 128:(t + 1) * 128, ocs * 512:(ocs + 1) * 512],
                                ztmpA[:, 0:512], sem="st", wait=ev_prev)
                        wo_guard[0] = tmark()
        phase_barrier()
        post_block(False, "xres", xres)
        phase_barrier()

    def l1_ffn():
        es2 = ExitStack()
        tpa = es2.enter_context(nc.psum_tensor("f1_tpa", [128, 1024], BF16))
        tpb = es2.enter_context(nc.psum_tensor("f1_tpb", [128, 1024], BF16))
        p1 = es2.enter_context(nc.psum_tensor("f1_p1", [128, 512], F32))
        p2 = es2.enter_context(nc.psum_tensor("f1_p2", [128, 512], F32))
        xqT = es2.enter_context(nc.sbuf_tensor("f1_xqT", [128, CT * TOK], BF16))
        w1b = es2.enter_context(nc.sbuf_tensor("f1_w1b", [128, 1, CT * 512], BF16))
        w2b = es2.enter_context(nc.sbuf_tensor("f1_w2b", [128, 1, CT * 512], BF16))
        hqT = es2.enter_context(nc.sbuf_tensor("f1_hqT", [128, FT * 256], BF16))
        w3b = es2.enter_context(nc.sbuf_tensor("f1_w3b", [128, 1, 32 * 512], BF16))
        gm = es2.enter_context(nc.sbuf_tensor("f1_gm", [128, 32], F32))
        gr = es2.enter_context(nc.sbuf_tensor("f1_gr", [128, 32], F32))
        gs = es2.enter_context(nc.sbuf_tensor("f1_gs", [128, 32], F32))
        hs = es2.enter_context(nc.sbuf_tensor("f1_hs", [128, 512], F32))
        hs2 = es2.enter_context(nc.sbuf_tensor("f1_hs2", [128, 512], F32))
        sgb = es2.enter_context(nc.sbuf_tensor("f1_sg", [128, 512], F32))
        hqs = es2.enter_context(nc.sbuf_tensor("f1_hqs", [128, 512], BF16))
        with es2:

            tp_begin(tpa, tpb)

            def quant_to(src_ap, n, dst_ap):
                ng = n // 64
                g3 = src_ap.rearrange("p (g e) -> p g e", e=64)
                V(lambda e, g3=g3, ng=ng: e.tensor_reduce(
                    out=gm[:, 0:ng], in_=g3, axis=mybir.AxisListType.X,
                    op=ALU.max, apply_absolute_value=True))
                vbar()
                V(lambda e, ng=ng: e.tensor_scalar(out=gm[:, 0:ng], in0=gm[:, 0:ng],
                                                   scalar1=EPS, scalar2=None, op0=ALU.max))
                vbar()
                V(lambda e, ng=ng: e.reciprocal(gr[:, 0:ng], gm[:, 0:ng]))
                vbar()
                V(lambda e, ng=ng: e.tensor_scalar(out=gr[:, 0:ng], in0=gr[:, 0:ng],
                                                   scalar1=127.0, scalar2=None, op0=ALU.mult))
                V(lambda e, ng=ng: e.tensor_scalar(out=gs[:, 0:ng], in0=gm[:, 0:ng],
                                                   scalar1=1.0 / 127.0, scalar2=None,
                                                   op0=ALU.mult))
                vbar()
                grb = gr[:, 0:ng].unsqueeze(2).broadcast_to([128, ng, 64])
                gsb = gs[:, 0:ng].unsqueeze(2).broadcast_to([128, ng, 64])
                q3 = ztmpB[:, 0:n].rearrange("p (g e) -> p g e", e=64)
                V(lambda e, g3=g3, grb=grb, q3=q3: e.tensor_tensor(
                    out=q3, in0=g3, in1=grb, op=ALU.mult))
                vbar()
                V(lambda e, n=n: e.tensor_scalar(out=junk_bf[:, 0:n].bitcast(F32) if False else xtile[:, 0:n], in0=ztmpB[:, 0:n],
                                                 scalar1=-127.0, scalar2=127.0,
                                                 op0=ALU.max, op1=ALU.min))
                vbar()
                V(lambda e, n=n: e.tensor_scalar(out=ztmpB[:, 0:n], in0=xtile[:, 0:n],
                                                 scalar1=MAGF, scalar2=MAGF,
                                                 op0=ALU.add, op1=ALU.subtract))
                vbar()
                d3 = dst_ap.rearrange("p (g e) -> p g e", e=64)
                V(lambda e, q3=q3, gsb=gsb, d3=d3: e.tensor_tensor(
                    out=d3, in0=q3, in1=gsb, op=ALU.mult))
                return vbar()

            ctx.wait("sync", ctx.tok("st"))
            cons = None
            for t in range(NT):
                tl = dma("sync", xtile[:, :], xres[t * 128:(t + 1) * 128, :], wait=cons)
                ctx.wait("vector", tl)
                stats_accum(xtile[:, :], t, C)
                cons = vbar()
            stats_finish(NT, EPS, C, st_mu, st_rs)
            for t in range(NT):
                tl = dma("sync", xtile[:, :], xres[t * 128:(t + 1) * 128, :], wait=cons)
                ctx.wait("vector", tl)
                V(lambda e, t=t: e.tensor_scalar(out=ztmpA[:, :], in0=xtile[:, :],
                                                 scalar1=st_mu[:, t:t + 1],
                                                 scalar2=st_rs[:, t:t + 1],
                                                 op0=ALU.subtract, op1=ALU.mult))
                vbar()
                ctx.wait("vector", ctx.tok("ts"))
                v_q = quant_to(ztmpA[:, :], C, xnbf[:, :])
                cons = v_q
                for j in range(CT):
                    ctx.wait("tensor", v_q)
                    tp_chunk(xnbf[:, j * 128:(j + 1) * 128],
                             xqT[:, j * TOK + t * 128:j * TOK + (t + 1) * 128])

            ctx.wait("vector", v_sscale)
            ctx.wait("scalar", v_sscale)

            def load_w12(dbuf, ag, half, ocs, guard):
                tok = None
                for g2 in range(2):
                    rows0 = g2 * 1024
                    src = ag.ap()[rows0:rows0 + 1024, ocs * 512:(ocs + 1) * 512]
                    tok = dma3("sync", dbuf[:, half, g2 * 8 * 512:(g2 + 1) * 8 * 512], src,
                               512, wait=(guard if g2 == 0 else None))
                return tok

            def load_w3(half, part, ocs, guard):
                rows0 = part * 4096
                src = ag_w3_out.ap()[rows0:rows0 + 4096, ocs * 512:(ocs + 1) * 512]
                return dma3("sync", w3b[:, half, :], src, 512, wait=guard)

            hq_guard = None
            hq_w3_guard = None
            for tg in range(4):
                tts = [2 * tg, 2 * tg + 1]
                ctx.wait("scalar", hq_w3_guard)
                w12_guard = [None]
                ev = None
                for ocs in range(CT):
                    w1_tok = load_w12(w1b, ag_w1_out, 0, ocs, w12_guard[0])
                    w2_tok = load_w12(w2b, ag_w2_out, 0, ocs, None)
                    ctx.wait("tensor", w1_tok)
                    ctx.wait("tensor", w2_tok)
                    for local, t in enumerate(tts):
                        ctx.wait("tensor", ev)
                        for j in range(CT):
                            mm(p1[:, :], xqT[:, j * TOK + t * 128:j * TOK + (t + 1) * 128],
                               w1b[:, 0, j * 512:(j + 1) * 512],
                               start=(j == 0), stop=(j == CT - 1))
                        t_1 = tmark()
                        for j in range(CT):
                            mm(p2[:, :], xqT[:, j * TOK + t * 128:j * TOK + (t + 1) * 128],
                               w2b[:, 0, j * 512:(j + 1) * 512],
                               start=(j == 0), stop=(j == CT - 1))
                        t_2 = tmark()
                        w12_guard[0] = t_2
                        ctx.wait("vector", t_1)
                        V(lambda e: e.tensor_scalar(out=hs[:, :], in0=p1[:, :],
                                                    scalar1=sscale[:, 0:1], scalar2=None,
                                                    op0=ALU.mult))
                        v_a = vbar()
                        ctx.wait("scalar", v_a)
                        S(lambda e: e.activation(out=sgb[:, :], in_=hs[:, :],
                                                 func=AF.Tanh, scale=0.5))
                        s_t = sdone()
                        ctx.wait("vector", s_t)
                        V(lambda e: e.tensor_scalar(out=hs2[:, :], in0=sgb[:, :],
                                                    scalar1=0.5, scalar2=0.5,
                                                    op0=ALU.mult, op1=ALU.add))
                        vbar()
                        V(lambda e: e.tensor_tensor(out=sgb[:, :], in0=hs[:, :],
                                                    in1=hs2[:, :], op=ALU.mult))
                        vbar()
                        ctx.wait("vector", t_2)
                        V(lambda e: e.tensor_scalar(out=hs2[:, :], in0=p2[:, :],
                                                    scalar1=sscale[:, 1:2], scalar2=None,
                                                    op0=ALU.mult))
                        vbar()
                        V(lambda e: e.tensor_tensor(out=hs[:, :], in0=sgb[:, :],
                                                    in1=hs2[:, :], op=ALU.mult))
                        ev = vbar()
                        ctx.wait("vector", hq_guard)
                        v_q = quant_to(hs[:, :], 512, hqs[:, :])
                        for jj in range(4):
                            j = ocs * 4 + jj
                            ctx.wait("tensor", v_q)
                            tp_chunk(hqs[:, jj * 128:(jj + 1) * 128],
                                     hqT[:, j * 256 + local * 128:j * 256 + (local + 1) * 128])
                        hq_guard = ctx.tok("ts")
                        ev = v_q
                e1, e2 = tp_flush()
                w3_guard = [None]
                ev3 = None
                for ocs in range(4):
                    for part in range(2):
                        w3_tok = load_w3(0, part, ocs, w3_guard[0])
                        ctx.wait("tensor", w3_tok)
                        ctx.wait("tensor", e1)
                        ctx.wait("tensor", e2)
                        for local, t in enumerate(tts):
                            pp = p1 if local % 2 == 0 else p2
                            if part == 0:
                                ctx.wait("tensor", ev3)
                            for jj in range(32):
                                j = part * 32 + jj
                                mm(pp[:, :],
                                   hqT[:, j * 256 + local * 128:j * 256 + (local + 1) * 128],
                                   w3b[:, 0, jj * 512:(jj + 1) * 512],
                                   start=(j == 0), stop=(j == FT - 1))
                            if part == 1:
                                t_k = tmark()
                                ctx.wait("vector", t_k)
                                ctx.wait("vector", ctx.tok("st"))
                                V(lambda e, pp=pp: e.tensor_scalar(
                                    out=ztmpA[:, 0:512], in0=pp[:, :],
                                    scalar1=sscale[:, 2:3], scalar2=None, op0=ALU.mult))
                                ev3 = vdone()
                                dma("sync",
                                    zsp[t * 128:(t + 1) * 128, ocs * 512:(ocs + 1) * 512],
                                    ztmpA[:, 0:512], sem="st", wait=ev3)
                        w3_guard[0] = tmark()
                hq_w3_guard = tmark()
        phase_barrier()
        return post_block(False, "xres", out_d)

    if DEBUG_STOP == "p0":
        dma("sync", out_d, xh[HALO:THL, :], sem="st", wait=ctx.tok("st"))
        ctx.wait("sync", ctx.tok("st"))
        ctx.wait("sync", ctx.tok("vs"))
        ctx.wait("sync", ctx.tok("ss"))
        ctx.wait("sync", ctx.tok("ts"))
        ctx.wait("sync", ctx.tok("cc"))
        ctx.wait("sync", ctx.tok("gd"))
        ctx.wait("sync", ctx.tok("ld"))
        with nc.Block() as block:
            @block.sync
            def _(eng):
                for fn in ctx.ops["sync"]:
                    fn(eng)

            @block.gpsimd
            def _(eng):
                for fn in ctx.ops["gpsimd"]:
                    fn(eng)

            @block.vector
            def _(eng):
                for fn in ctx.ops["vector"]:
                    fn(eng)

            @block.scalar
            def _(eng):
                for fn in ctx.ops["scalar"]:
                    fn(eng)

            @block.tensor
            def _(eng):
                for fn in ctx.ops["tensor"]:
                    fn(eng)
        es.close()
        return nc
    l0_attn()
    if DEBUG_STOP == "l0a:ln":
        dma("sync", out_d[0:128, 0:16], st_mu[:, :], sem="st", wait=ctx.tok("st"))
        dma("sync", out_d[0:128, 16:32], st_rs[:, :], sem="st")
        dma("sync", out_d[0:128, 32:48], st_s1[:, :], sem="st")
        dma("sync", out_d[0:128, 48:64], st_s2[:, :], sem="st")
        dma("sync", out_d[0:128, 64:80], st_var[:, :], sem="st")
    if DEBUG_STOP == "l0a:xnt":
        dma("sync", out_d[0:640, :].bitcast(BF16), xnT_sp[:, :], sem="st",
            wait=ctx.tok("st"))
    elif DEBUG_STOP == "l0a:wkv":
        dma("sync", out_d[0:512, :].bitcast(BF16), wkv_sp[:, :], sem="st",
            wait=ctx.tok("st"))
    elif DEBUG_STOP == "l0a:att":
        pass
    if DEBUG_STOP is not None and DEBUG_STOP.startswith("l0a"):
        if DEBUG_STOP not in ("l0a:xnt", "l0a:wkv", "l0a:ln", "l0a:att", "l0a:attg", "l0a:attb", "l0a:attw", "l0a:attr", "l0a:B1d", "l0a:B1m"):
            dma("sync", out_d, xres, sem="st", wait=ctx.tok("st"))
    else:
        l0_ffn()
        if DEBUG_STOP == "l0f":
            dma("sync", out_d, xres, sem="st", wait=ctx.tok("st"))
        else:
            l1_attn()
            if DEBUG_STOP == "l1a":
                dma("sync", out_d, xres, sem="st", wait=ctx.tok("st"))
            else:
                l1_ffn()

    ctx.wait("sync", ctx.tok("st"))
    ctx.wait("sync", ctx.tok("vs"))
    ctx.wait("sync", ctx.tok("ss"))
    ctx.wait("sync", ctx.tok("ts"))
    ctx.wait("sync", ctx.tok("cc"))
    ctx.wait("sync", ctx.tok("gd"))
    ctx.wait("sync", ctx.tok("ld"))

    with nc.Block() as block:
        @block.sync
        def _(eng):
            for fn in ctx.ops["sync"]:
                fn(eng)

        @block.gpsimd
        def _(eng):
            for fn in ctx.ops["gpsimd"]:
                fn(eng)

        @block.vector
        def _(eng):
            for fn in ctx.ops["vector"]:
                fn(eng)

        @block.scalar
        def _(eng):
            for fn in ctx.ops["scalar"]:
                fn(eng)

        @block.tensor
        def _(eng):
            for fn in ctx.ops["tensor"]:
                fn(eng)

    es.close()
    return nc


def _stage_inputs(inputs):
    f = lambda k: np.ascontiguousarray(np.asarray(inputs[k], dtype=np.float32))
    x = f("L0_attn_x")
    v16names = {"tmk": "L0_attn_tm_k", "tmv": "L0_attn_tm_v", "tmr": "L0_attn_tm_r",
                "tmg": "L0_attn_tm_g", "td": "L0_attn_time_decay", "tf": "L0_attn_time_first"}
    wTmap = {"wr": "L0_attn_Wr", "wk0": "L0_attn_Wk", "wv0": "L0_attn_Wv",
             "wg": "L0_attn_Wg", "wo0": "L0_attn_Wo", "wq": "L1_attn_Wq",
             "wk1": "L1_attn_Wk", "wv1": "L1_attn_Wv", "wo1": "L1_attn_Wo"}
    wT_np = {k: np.ascontiguousarray(f(v).T) for k, v in wTmap.items()}
    w1T = np.ascontiguousarray(f("L1_ffn_w1").T)
    w2T = np.ascontiguousarray(f("L1_ffn_w2").T)
    w3T = np.ascontiguousarray(f("L1_ffn_w3").T)
    wfc = f("L0_ffn_W_fc")
    wpj = f("L0_ffn_W_proj")
    per_core = []
    for c in range(NC):
        b, h = c // 2, c % 2
        m = {}
        if h == 0:
            halo = np.zeros((HALO, C), np.float32)
            halo[-1, :] = x[b, 1, :]
            m["xh"] = np.ascontiguousarray(np.concatenate([halo, x[b, 0:TOK]], axis=0))
            hm = np.ones((128, THL), np.float32)
            hm[:, 0:HALO] = 0.0
            m["hmask"] = hm
            m["pmbB"] = np.full((128, 1), -60020.0, np.float32)
            m["wsA"] = np.ones((128, 1), np.float32)
            m["wsB"] = np.zeros((128, 1), np.float32)
        else:
            m["xh"] = np.ascontiguousarray(x[b, TOK - HALO:2 * TOK])
            m["hmask"] = np.ones((128, THL), np.float32)
            m["pmbB"] = np.full((128, 1), -20.0, np.float32)
            m["wsA"] = np.zeros((128, 1), np.float32)
            m["wsB"] = np.ones((128, 1), np.float32)
        for nm, key in v16names.items():
            m[nm] = np.ascontiguousarray(f(key).reshape(CT, 128).T)
        m["lng_row"] = f("L0_attn_ln_g")[None, :]
        m["lnb_row"] = f("L0_attn_ln_b")[None, :]
        m["gq_row"] = np.ascontiguousarray(np.tile(f("L1_attn_gq"), CT)[None, :])
        m["gk_row"] = np.ascontiguousarray(np.tile(f("L1_attn_gk"), CT)[None, :])
        for k, v in wT_np.items():
            m[k + "T"] = v
        m["fc_sh"] = np.ascontiguousarray(wfc[c * 1024:(c + 1) * 1024])
        m["pj_sh"] = np.ascontiguousarray(wpj[c * 256:(c + 1) * 256])
        m["w1T_sh"] = np.ascontiguousarray(w1T[c * 256:(c + 1) * 256])
        m["w2T_sh"] = np.ascontiguousarray(w2T[c * 256:(c + 1) * 256])
        m["w3T_sh"] = np.ascontiguousarray(w3T[c * 1024:(c + 1) * 1024])
        per_core.append(m)
    return per_core


def kernel(**inputs):
    nc = build_program()
    in_maps = _stage_inputs(inputs)
    res = run_bass_kernel_spmd(nc, in_maps, core_ids=list(range(NC)))
    outs = [np.asarray(r["out"], dtype=np.float32) for r in res.results]
    full = np.zeros((B, T, C), np.float32)
    for c in range(NC):
        b, h = c // 2, c % 2
        full[b, h * TOK:(h + 1) * TOK, :] = outs[c]
    return full
